# version 08 (final)
# baseline (speedup 1.0000x reference)
import contextlib
import numpy as np
import ml_dtypes
import concourse.bass as bass
import concourse.mybir as mybir
from concourse.bass_utils import run_bass_kernel_spmd

F32 = mybir.dt.float32
BF16 = mybir.dt.bfloat16
AF = mybir.ActivationFunctionType
ALU = mybir.AluOpType
AX = mybir.AxisListType

H, HH, HD, D, C = 16, 8, 64, 1024, 128
DC = D // 128
NPAIR = HH // 2

DEBUG = []
PHASE = [None]
SERIAL = [False]
PADN = [0, "vector"]
PARANOID = [set()]
EPS = 1e-5



def _jdev():
    import jax
    return jax.local_devices(backend="cpu")[0]


def make_xpos_tables(S):
    import jax
    import jax.numpy as jnp
    ROT = 32
    with jax.default_device(_jdev()):
        half = jnp.arange(0, ROT, 2, dtype=jnp.float32)
        inv_freq = 1.0 / 10000.0 ** (half / ROT)
        t = jnp.arange(S, dtype=jnp.float32)
        freqs = jnp.repeat(t[:, None] * inv_freq, 2, axis=-1)
        scale_vals = (half + 0.4 * ROT) / (1.4 * ROT)
        power = (t - S // 2) / 512.0
        scale = scale_vals[None, :] ** power[:, None]
        scale = jnp.concatenate((scale, scale), axis=-1)
        cos, sin = jnp.cos(freqs), jnp.sin(freqs)
        aq, bq = np.asarray(cos * scale), np.asarray(sin * scale)
        ak, bk = np.asarray(cos / scale), np.asarray(sin / scale)
    sgn = np.where(np.arange(32) % 2 == 0, np.float32(-1.0), np.float32(1.0))
    res = []
    for i, a in enumerate((aq, bq, ak, bk)):
        full = np.zeros((S, 64), np.float32)
        full[:, 32:] = 1.0 if i % 2 == 0 else 0.0
        full[:, :32] = a if i % 2 == 0 else a * sgn
        res.append(full)
    return res


def make_gamma_pows(maxp):
    import jax
    import jax.numpy as jnp
    with jax.default_device(_jdev()):
        gam = 1.0 - 2.0 ** (-5.0 - jnp.arange(H, dtype=jnp.float32))
        n = jnp.arange(maxp + 1, dtype=jnp.float32)
        pw = gam[:, None] ** n[None, :]
        return np.asarray(gam), np.asarray(pw)


def make_ret_tables(heads):
    _, pw = make_gamma_pows(C + 1)
    DT = np.zeros((C, HH * C), np.float32)
    KS = np.zeros((C, HH * HD), np.float32)
    QS = np.zeros((64, HH * C), np.float32)
    GR = np.zeros((64, HH * HD), np.float32)
    ii = np.arange(C)
    for hh, h in enumerate(heads):
        dif = ii[None, :] - ii[:, None]
        DT[:, C * hh:C * (hh + 1)] = np.where(dif >= 0, pw[h][dif.clip(0)], 0.0)
        KS[:, HD * hh:HD * (hh + 1)] = pw[h][C - 1 - ii][:, None]
        QS[:, C * hh:C * (hh + 1)] = pw[h][ii + 1][None, :]
        GR[:, HD * hh:HD * (hh + 1)] = pw[h][C]
    return DT, KS, QS, GR


def make_p2swap():
    P = np.zeros((128, 128), np.float32)
    for b in range(2):
        o = 64 * b
        for i in range(0, 32, 2):
            P[o + i, o + i + 1] = 1.0
            P[o + i + 1, o + i] = 1.0
        for i in range(32, 64):
            P[o + i, o + i] = 1.0
    return P.astype(ml_dtypes.bfloat16)


def make_hsel():
    M = np.zeros((128, 2), np.float32)
    M[0:64, 0] = 1.0
    M[64:128, 1] = 1.0
    return M


def make_parsel():
    M = np.zeros((2, 128), np.float32)
    M[0, 0:64] = 1.0
    M[1, 64:128] = 1.0
    return M



class Sem:
    def __init__(self, handle, name):
        self.h, self.name, self.val = handle, name, 0

    def tok(self):
        return (self, self.val)


class SemG:

    def __init__(self, sems):
        self.sems = sems
        self.i = 0

    def next(self):
        sm = self.sems[self.i % len(self.sems)]
        self.i += 1
        return sm

    def tok(self):
        return [sm.tok() for sm in self.sems]


class Sched:
    ENGINES = ("sync", "tensor", "vector", "scalar", "gpsimd")

    def __init__(self):
        self.q = {e: [] for e in self.ENGINES}
        self.seen = {e: {} for e in self.ENGINES}
        self.n = 0
        self.dead = False

    def op(self, eng, fn=None, waits=(), inc=None, amt=1):
        if self.dead:
            return ("DEAD", 0)
        waits = tuple(waits)
        if inc is not None:
            inc.val += amt
            tok = (inc, inc.val)
        else:
            tok = None
        seen = self.seen[eng]
        self.n += 1

        def run(engobj, waits=waits, inc=inc, amt=amt, fn=fn):
            for sm, v in waits:
                if seen.get(sm.name, 0) < v:
                    engobj.wait_ge(sm.h, v)
                    seen[sm.name] = v
            if fn is None:
                if inc is not None:
                    engobj.nop().then_inc(inc.h, amt)
                return
            inst = fn(engobj)
            if inc is not None:
                inst.then_inc(inc.h, amt)
        self.q[eng].append(run)
        return tok

    def emit(self, block):
        for name in self.ENGINES:
            ops = self.q[name]

            def body(engobj, ops=ops):
                for f in ops:
                    f(engobj)
            getattr(block, name)(body)


class CFG:
    def __init__(self, T=2048, S=4096, F=4096, n_cores=8):
        self.T, self.S, self.F, self.n_cores = T, S, F, n_cores
        self.FT = min(256, T)
        self.NFT = T // self.FT
        self.AT = min(512, T)
        self.NAT = T // self.AT
        self.FC = F // 128
        self.NCG = S // C
        self.groups = [[2 * i, 2 * i + 1] for i in range(max(1, n_cores // 2))]



def build(cfg: CFG):
    T, S, F = cfg.T, cfg.S, cfg.F
    FT, NFT, AT, NAT, FCn, NCG = (cfg.FT, cfg.NFT, cfg.AT, cfg.NAT,
                                  cfg.FC, cfg.NCG)
    nc = bass.Bass("TRN2", target_bir_lowering=False, debug=False)
    s = Sched()
    st_ = contextlib.ExitStack()
    A = st_.enter_context

    def din(name, shape, dt=F32):
        return nc.dram_tensor(name, shape, dt, kind="ExternalInput").ap()

    x_in = din("x_in", [T, D])
    wfc_in = din("wfc", [D, F])
    wproj_in = din("wproj", [F, D])
    flnw_in = din("flnw", [F, 1])
    flnb_in = din("flnb", [F, 1])
    wqkv_in = din("wqkv", [D, 3 * D])
    ln1w_in = din("ln1w", [D, 1])
    ln1b_in = din("ln1b", [D, 1])
    wg_in = din("wg", [D, D])
    wo_in = din("wo", [D, D])
    w1_in = din("w1", [D, D])
    w1b_in = din("w1b", [D, 1])
    w2_in = din("w2", [D, D])
    w2b_in = din("w2b", [D, 1])
    ln2w_in = din("ln2w", [D, 1])
    ln2b_in = din("ln2b", [D, 1])
    gnv_in = din("gnv", [2, 2 * NPAIR])
    taq_in = din("taq", [128, T])
    tbq_in = din("tbq", [128, T])
    tak_in = din("tak", [128, T])
    tbk_in = din("tbk", [128, T])
    dt_in = din("dtm", [C, HH * C])
    ks_in = din("ksm", [C, HH * HD])
    qs_in = din("qsm", [64, HH * C])
    gr_in = din("grm", [64, HH * HD])
    ident_in = din("identm", [128, 128])
    identb_in = din("identbm", [128, 128], BF16)
    p2_in = din("p2m", [128, 128], BF16)
    ones_in = din("onesm", [128, 128])
    hsel_in = din("hselm", [128, 2])
    parsel_in = din("parselm", [2, 128])
    pmask_in = din("pmask", [128, 2])

    out_t = nc.dram_tensor("out", [T, D], F32, kind="ExternalOutput").ap()
    dbg = {}
    for name, shape in (("x2", [D, T]), ("qkv", [3 * HH * HD, S]),
                        ("ret", [HH * HD, S]), ("y2", [D, T]), ("x3", [D, T]),
                        ("xs1", [D, T]), ("h1", [F, T]), ("hs1", [F, T]),
                        ("st1", [2, T]), ("thr", [128, 8])):
        if name in DEBUG:
            dbg[name] = nc.dram_tensor("dbg_" + name, shape, F32,
                                       kind="ExternalOutput").ap()

    def dscr(name, shape, dt=F32):
        return nc.dram_tensor(name, shape, dt, kind="Internal").ap()

    xh = dscr("xh", [D, T])
    y2h = dscr("y2h", [D, T])
    wqf_sp = dscr("wqf_sp", [D, F], BF16)
    wqp_sp = dscr("wqp_sp", [F, D], BF16)
    cc1_in = dscr("cc1_in", [3 * D, S], BF16)
    cc1_out = dscr("cc1_out", [3 * HH * HD, S], BF16)
    cc2_in = dscr("cc2_in", [4 * HH * HD, T], BF16)
    cc2_out = dscr("cc2_out", [2 * HH * HD, T], BF16)

    sems = {}
    for nm in ("pe", "ve", "se", "ge", "sy", "cc"):
        sems[nm] = Sem(A(nc.semaphore(nm)), nm)

    def mkgroup(nm, k):
        return SemG([Sem(A(nc.semaphore(f"{nm}{i}")), f"{nm}{i}")
                     for i in range(k)])
    ld, lw, sp, rb, so, gq = (mkgroup(nm, k) for nm, k in
                              (("ld", 4), ("lw", 4), ("sp", 4), ("rb", 4),
                               ("so", 2), ("gq", 4)))
    dma_groups = [ld, lw, sp, rb, so, gq]
    pe, ve, se, ge, sy, ccm = (sems[k] for k in ("pe", "ve", "se", "ge", "sy", "cc"))
    ENG_SEM = {"tensor": pe, "vector": ve, "scalar": se, "gpsimd": ge, "sync": sy}

    def now(*engs):
        return [ENG_SEM[e].tok() for e in engs]

    def barrier():
        toks = [s.op(e, None, waits=[ENG_SEM[e].tok()], inc=ENG_SEM[e])
                for e in Sched.ENGINES]
        dtoks = [t for g in dma_groups for t in g.tok()]
        for e in Sched.ENGINES:
            s.op(e, None, waits=[t for t in toks
                                 if t[0] is not ENG_SEM[e]] + dtoks)

    def cut(k):
        if PHASE[0] is None or PHASE[0] > k or s.dead:
            return
        if PADN[1] == "acttab":
            for i_ in range(PADN[0]):
                fn_ = AF.Gelu if i_ % 2 == 0 else AF.Sqrt
                s.op("scalar", lambda e, fn_=fn_: e.activation(
                    lnt1[:, 0:8], sxb[:, 0:8], fn_), waits=[(se, se.val)],
                    inc=se)
        elif PADN[1] == "mm":
            pv = None
            for i_ in range(PADN[0]):
                pm = MM(PS[0:1, 2560:2568], ones_bf[0:1, 0:1],
                        ones_bf[0:1, 0:8], waits=[pv] if pv else [(ve, ve.val)])
                pv = V(lambda e: e.tensor_copy(lnt1[0:1, 0:8],
                                               PS[0:1, 2560:2568]), waits=[pm])
        else:
            for _ in range(PADN[0]):
                s.op(PADN[1], lambda e: e.nop())
        s.op("gpsimd", None, waits=[(ccm, ccm.val)])
        barrier()
        tok = dma(out_t.rearrange("a b -> (a b)"),
                  xh.rearrange("a b -> (a b)"), sem=so)
        s.op("sync", None, waits=[tok])
        s.op("gpsimd", None, waits=[tok])
        s.dead = True

    ident = A(nc.sbuf_tensor("ident", [128, 128], F32))
    identb = A(nc.sbuf_tensor("identb", [128, 128], BF16))
    ones = A(nc.sbuf_tensor("ones", [128, 128], F32))
    ones_bf = A(nc.sbuf_tensor("ones_bf", [128, 128], BF16))
    p2 = A(nc.sbuf_tensor("p2", [128, 128], BF16))
    hsel = A(nc.sbuf_tensor("hsel", [128, 2], F32))
    parsel = A(nc.sbuf_tensor("parsel", [2, 128], F32))
    ROWT = A(nc.sbuf_tensor("ROWT", [1, 6 * 512], F32))
    ROWB = A(nc.sbuf_tensor("ROWB", [1, 6 * 512], BF16))
    vecs = A(nc.sbuf_tensor("vecs", [128, 256], F32))
    sxb = A(nc.sbuf_tensor("sxb", [128, 512], BF16))
    ssq = A(nc.sbuf_tensor("ssq", [128, 512], BF16))
    lnt1 = A(nc.sbuf_tensor("lnt1", [128, 512], BF16))
    PS = A(nc.psum_tensor("PS", [128, 3584], F32))
    PSB = A(nc.psum_tensor("PSB", [128, 1024], BF16))

    BK = [PS[:, 512 * i:512 * (i + 1)] for i in range(7)]

    rslot, vslot = {}, {}

    def rT(name, w=512):
        if name not in rslot:
            rslot[name] = len(rslot)
            assert len(rslot) <= 6, "row slot overflow"
        i = rslot[name]
        return ROWT[0:1, 512 * i:512 * i + w]

    def rB(name, w=512):
        if name not in rslot:
            rslot[name] = len(rslot)
            assert len(rslot) <= 6, "row slot overflow"
        i = rslot[name]
        return ROWB[0:1, 512 * i:512 * i + w]

    def vcol(name, n=1):
        if name not in vslot:
            i = (max(i0 + w0 for (i0, w0) in vslot.values()) if vslot else 0)
            vslot[name] = (i, n)
            assert i + n <= 256, "vecs arena overflow"
        i, w = vslot[name]
        return vecs[:, i:i + w]

    def dma(dst, src, sem=None, waits=(), eng="sync", slow=False):
        g = sem if sem is not None else ld
        sm = g.next() if isinstance(g, SemG) else g
        return s.op(eng, lambda e, d=dst, so=src, sl=slow: e.dma_start(
            out=d, in_=so, allow_slow_non_contiguous=sl),
            waits=list(waits) + [sm.tok()], inc=sm, amt=16)

    def _para(waits, tag):
        if PARANOID[0] and ("all" in PARANOID[0] or tag in PARANOID[0]):
            return list(waits) + [(ve, ve.val), (se, se.val)]
        return waits

    def MM(out, lhsT, rhs, start=True, stop=True, waits=(), inc=pe, tag=None):
        waits = _para(waits, tag)
        return s.op("tensor", lambda e, o=out, l=lhsT, r=rhs, a=start, b=stop:
                    e.matmul(o, l, r, start=a, stop=b), waits=waits, inc=inc)

    def TRP(out, in_, idn, waits=(), inc=pe, tag=None):
        waits = _para(waits, tag)
        return s.op("tensor", lambda e, o=out, i=in_, d=idn: e.transpose(o, i, d),
                    waits=waits, inc=inc)

    def V(fn, waits=(), inc=ve):
        return s.op("vector", fn, waits=waits, inc=inc)

    def SC(fn, waits=(), inc=se):
        return s.op("scalar", fn, waits=waits, inc=inc)

    def G(fn, waits=(), inc=ge):
        return s.op("gpsimd", fn, waits=list(waits) + [(ge, ge.val)], inc=inc)

    def bcast(src_bf, dst_ps, w, waits):
        tk = None
        for j in range(0, w, 512):
            wj = min(512, w - j)
            tk = MM(dst_ps[:, j:j + wj], ones_bf[0:1, :],
                    src_bf[0:1, j:j + wj], waits=list(waits), tag="bcast")
        return tk

    cst = vcol("cstc", 2)
    s.op("vector", lambda e: e.memset(cst[:, 0:1], EPS), inc=ve)
    s.op("vector", lambda e: e.memset(cst[:, 1:2], 1.0), inc=ve)
    nc.const_aps.register(F32, EPS, cst[:, 0:1])
    nc.const_aps.register(F32, 1.0, cst[:, 1:2])

    pmk = vcol("pmk", 2)
    tload = []
    for d_, so_ in ((ident, ident_in), (identb, identb_in), (ones, ones_in),
                    (p2, p2_in), (hsel, hsel_in), (parsel, parsel_in),
                    (pmk, pmask_in)):
        tload.append(dma(d_[:], so_))
    t_onesbf = V(lambda e: e.tensor_copy(ones_bf[:], ones[:]), waits=tload)

    ffn1_sc = contextlib.ExitStack()
    wqf = [ffn1_sc.enter_context(nc.sbuf_tensor(f"wqf{i}", [128, F], BF16))
           for i in range(DC)]
    wqp = [ffn1_sc.enter_context(nc.sbuf_tensor(f"wqp{i}", [128, D], BF16))
           for i in range(FCn)]
    qq_sc = contextlib.ExitStack()
    wst = [qq_sc.enter_context(nc.sbuf_tensor(f"wst{i}", [128, max(F, D)], F32))
           for i in range(2)]
    flnw = qq_sc.enter_context(nc.sbuf_tensor("flnwS", [128, FCn], F32))
    flnb = qq_sc.enter_context(nc.sbuf_tensor("flnbS", [128, FCn], F32))
    flnbB = qq_sc.enter_context(nc.sbuf_tensor("flnbB", [128, FCn], BF16))
    tw1 = dma(flnw[:, :], flnw_in.rearrange("(c p) o -> p (c o)", p=128), slow=True)
    tw2 = dma(flnb[:, :], flnb_in.rearrange("(c p) o -> p (c o)", p=128), slow=True)
    tw2 = V(lambda e: e.tensor_copy(flnbB[:, :], flnb[:, :]), waits=[tw2])

    thr_tok = {}
    for wi, (src, nch, width) in enumerate([(wfc_in, DC, F), (wproj_in, FCn, D)]):
        acc = vcol(f"qacc{wi}", 34)
        vt, lt = [], []
        for cch in range(nch):
            buf = wst[cch % 2]
            w_ = [vt[-2]] if len(vt) >= 2 else [(ve, ve.val)]
            lt.append(dma(buf[:, :width], src[128 * cch:128 * (cch + 1), :],
                          waits=w_))
            vt.append(V(lambda e, b=buf, c=cch, width=width, acc=acc:
                        e.reduce_sum(acc[:, c:c + 1], b[:, :width], axis=AX.X,
                                     apply_absolute_value=True), waits=[lt[-1]]))
        tsum = V(lambda e, acc=acc, nch=nch: e.reduce_sum(
            acc[:, 32:33], acc[:, 0:nch], axis=AX.X), waits=[vt[-1]])
        ttot = MM(PS[0:1, 3072 + wi:3073 + wi], ones[:, 0:1], acc[:, 32:33],
                  waits=[tsum])
        tth = V(lambda e, wi=wi, nel=nch * 128 * width: e.tensor_scalar(
            rT("s")[0:1, 64 + wi:65 + wi], PS[0:1, 3072 + wi:3073 + wi],
            0.5 / nel, 0.5e-6, ALU.mult, ALU.add), waits=[ttot])
        tbb = MM(PS[:, 3076 + wi:3077 + wi], ones[0:1, :],
                 rT("s")[0:1, 64 + wi:65 + wi], waits=[tth])
        thp = vcol(f"thr{wi}", 2)
        tc1 = V(lambda e, thp=thp, wi=wi: e.tensor_copy(
            thp[:, 0:1], PS[:, 3076 + wi:3077 + wi]), waits=[tbb])
        thr_tok[wi] = V(lambda e, thp=thp: e.tensor_scalar(
            thp[:, 1:2], thp[:, 0:1], -1.0, None, ALU.mult), waits=[tc1])
        if "thr" in dbg:
            dma(dbg["thr"][:, 4 * wi:4 * wi + 2], thp[:, 0:2], sem=so,
                waits=[thr_tok[wi]])
            dma(dbg["thr"][:, 4 * wi + 2:4 * wi + 3],
                PS[:, 3076 + wi:3077 + wi], sem=so,
                waits=[thr_tok[wi]]) if False else None

    b2ps_cols = PS[:, 3104:3104 + DC]
    for wi, (src, nch, width, dst, dsp) in enumerate(
            [(wfc_in, DC, F, wqf, wqf_sp), (wproj_in, FCn, D, wqp, wqp_sp)]):
        thp = vcol(f"thr{wi}")
        spt = []
        for cch in range(nch):
            buf = wst[cch % 2]
            w_ = [thr_tok[wi], (ve, ve.val) if not spt else spt[0]]
            if len(spt) >= 2:
                w_[1] = spt[-2]
            tl = dma(buf[:, :width], src[128 * cch:128 * (cch + 1), :], waits=w_)
            tn = V(lambda e, b=buf, c=cch, width=width, dst=dst, thp=thp:
                   e.tensor_single_scalar(dst[c][:, :width], b[:, :width],
                                          thp[:, 1:2], ALU.is_lt), waits=[tl])
            tq = V(lambda e, b=buf, c=cch, width=width, dst=dst, thp=thp:
                   e.scalar_tensor_tensor(dst[c][:, :width], b[:, :width],
                                          thp[:, 0:1], dst[c][:, :width],
                                          ALU.is_gt, ALU.subtract), waits=[tn])
            last = tq
            if wi == 1:
                bmt = []
                for mo in range(DC):
                    bmt.append(MM(b2ps_cols[:, mo:mo + 1],
                                  dst[cch][:, 128 * mo:128 * (mo + 1)],
                                  flnbB[:, cch:cch + 1],
                                  start=(cch == 0 and mo == 0),
                                  stop=(cch == nch - 1 and mo == DC - 1),
                                  waits=[tq, tw2] +
                                        (list(thr_tok.values())
                                         if cch == 0 and mo == 0 else [])))
                last = V(lambda e, c=cch, dst=dst, width=width: e.tensor_scalar(
                    dst[c][:, :width], dst[c][:, :width], flnw[:, c:c + 1],
                    None, ALU.mult), waits=[tq, tw1, bmt[-1]])
            spt.append(dma(dsp[128 * cch:128 * (cch + 1), :],
                           dst[cch][:, :width], waits=[last], sem=gq,
                           eng="gpsimd"))
    b2h = vcol("b2h", DC)
    V(lambda e: e.tensor_scalar(b2h[:, 0:DC], b2ps_cols[:, 0:DC], 0.5, None,
                                ALU.mult), waits=now("tensor"))
    qq_sc.close()
    cut(1)


    def stats2(xt, w, extra_waits=(), nch=None):
        nch = len(xt)
        tprev = None
        m1 = m2 = None
        for cch in range(nch):
            t1 = V(lambda e, c=cch: e.tensor_copy(sxb[:, :w], xt[c]),
                   waits=list(extra_waits) + ([tprev] if tprev else []) +
                         ([m1] if m1 else []))
            t2 = V(lambda e: e.tensor_mul(ssq[:, :w], sxb[:, :w], sxb[:, :w]),
                   waits=[t1] + ([m2] if m2 else []))
            m1 = MM(PS[0:1, 2560:2560 + w], ones_bf[:, 0:1], sxb[:, :w],
                    start=(cch == 0), stop=(cch == nch - 1), waits=[t1],
                    tag="stats")
            m2 = MM(PS[0:1, 3072:3072 + w],
                    ones_bf[:, 0:1], ssq[:, :w],
                    start=(cch == 0), stop=(cch == nch - 1), waits=[t2],
                    tag="stats")
            tprev = t2
        tA = V(lambda e: e.tensor_copy(rT("s", w), PS[0:1, 2560:2560 + w]),
               waits=[m1, (ge, ge.val), (se, se.val)] + so.tok())
        tB = V(lambda e: e.tensor_copy(rT("q", w), PS[0:1, 3072:3072 + w]),
               waits=[m2, tA])
        return tB

    def ln_rows(w, waits, denom):
        t1 = G(lambda e: e.tensor_scalar(rT("mu", w), rT("s", w), 1.0 / denom,
                                         None, ALU.mult), waits=waits)
        t2 = G(lambda e: e.tensor_mul(rT("sg", w), rT("mu", w), rT("mu", w)),
               waits=[t1])
        t3 = V(lambda e: e.scalar_tensor_tensor(rT("sg", w), rT("q", w),
                                                1.0 / denom, rT("sg", w),
                                                ALU.mult, ALU.subtract),
               waits=[t2, (ge, ge.val)])
        t4 = SC(lambda e: e.activation(rT("sg", w), rT("sg", w), AF.Sqrt,
                                       bias=EPS, scale=1.0), waits=[t3])
        t5 = V(lambda e: e.reciprocal(rT("sg", w), rT("sg", w)), waits=[t4])
        t6 = G(lambda e: e.tensor_mul(rT("nm", w), rT("mu", w), rT("sg", w)),
               waits=[t5])
        t7 = G(lambda e: e.tensor_scalar(rT("nm", w), rT("nm", w), -1.0, None,
                                         ALU.mult), waits=[t6])
        t8 = G(lambda e: e.tensor_copy(rB("sg", w), rT("sg", w)), waits=[t5])
        t9 = G(lambda e: e.tensor_copy(rB("nm", w), rT("nm", w)), waits=[t7])
        return [t8, t9]

    def apply_ln(xt, out_bf, s_ap, nm_ap, w, waits):
        waits = list(waits) + [(ve, ve.val)]
        tb1 = bcast(s_ap, BK[3], w, waits)
        tb2 = bcast(nm_ap, BK[4], w, waits)
        tprev = None
        toks = []
        for cch in range(len(xt)):
            t1 = V(lambda e, c=cch: e.tensor_mul(lnt1[:, :w], xt[c], BK[3][:, :w]),
                   waits=[tb1] + ([tprev] if tprev else []))
            tprev = V(lambda e, c=cch: e.tensor_add(out_bf[c], lnt1[:, :w],
                                                    BK[4][:, :w]),
                      waits=[tb2, t1])
            toks.append(tprev)
        return toks


    def emit_ffn(tag, src_fm, wqf, wqp, sink_dbg=None, init_waits=()):
        init_waits = list(init_waits)
        f_sc = contextlib.ExitStack()
        XT = [f_sc.enter_context(nc.sbuf_tensor(f"XT{tag}{i}", [128, FT], F32))
              for i in range(DC)]
        XS = [f_sc.enter_context(nc.sbuf_tensor(f"XS{tag}{i}", [128, FT], BF16))
              for i in range(DC)]
        HB = [f_sc.enter_context(nc.sbuf_tensor(f"HB{tag}{i}", [128, FT], BF16))
              for i in range(FCn)]
        XTM = f_sc.enter_context(nc.sbuf_tensor(f"XTM{tag}", [128, D], F32))
        EE = f_sc.enter_context(nc.sbuf_tensor(f"EE{tag}", [128, FT], F32))

        prev_store = []
        for t in range(NFT):
            tofs = t * FT
            if SERIAL[0]:
                barrier()
            if not src_fm:
                tr_ev = []
                for sub in range(FT // 128):
                    tl = dma(XTM[:, :],
                             x_in[tofs + 128 * sub:tofs + 128 * (sub + 1), :],
                             waits=(tr_ev[-1:] or prev_store) + init_waits)
                    for i in range(DC):
                        pslice = BK[2][:, 128 * (i % 2):128 * (i % 2) + 128]
                        pt = TRP(pslice, XTM[:, 128 * i:128 * (i + 1)], ident[:],
                                 waits=[tl] + ([tr_ev[-1]]
                                               if len(tr_ev) >= 1 else []),
                                 tag="trp")
                        tr_ev.append(V(lambda e, i=i, sub=sub, p=pslice:
                                       e.tensor_copy(
                                           XT[i][:, 128 * sub:128 * (sub + 1)], p),
                                       waits=[pt]))
                tload = [tr_ev[-1]]
            else:
                tload = [dma(XT[i][:, :], xh[128 * i:128 * (i + 1), tofs:tofs + FT],
                             waits=prev_store + init_waits)
                         for i in range(DC)]

            if tag == "a":
                cut(1.0001 + 0.01 * t)
            tq = stats2([XT[i][:, :FT] for i in range(DC)], FT,
                        extra_waits=tload + [(ve, ve.val)])
            if tag == "a":
                cut(1.0002 + 0.01 * t)
            rmt = ln_rows(FT, [tq], D)
            if tag == "a":
                cut(1.0003 + 0.01 * t)
            at = apply_ln([XT[i][:, :FT] for i in range(DC)],
                          [XS[i][:, :FT] for i in range(DC)],
                          rB("sg", FT), rB("nm", FT), FT, waits=rmt)

            if tag == "a" and "xs1" in dbg:
                for i in range(DC):
                    dma(dbg["xs1"][128 * i:128 * (i + 1), tofs:tofs + FT],
                        XS[i][:, :FT], sem=gq, waits=[at[-1]], eng="gpsimd")
            if tag == "a" and "st1" in dbg:
                tscp = V(lambda e: e.tensor_copy(sxb[:, 0:2].bitcast(F32)
                                                 if False else ssq[:, 0:2],
                                                 ssq[:, 0:2]), waits=[])
                dma(dbg["st1"][0:1, tofs:tofs + FT], rT("s", FT), sem=so,
                    waits=rmt)
                dma(dbg["st1"][1:2, tofs:tofs + FT], rT("q", FT), sem=so,
                    waits=rmt)
            if tag == "a":
                cut(1.0004 + 0.01 * t)
            fcev = []
            for m in range(FCn):
                bank = BK[m % 2]
                w_ = [at[-1]]
                if len(fcev) >= 2:
                    w_.append(fcev[m - 2])
                mt = None
                for kc in range(DC):
                    mt = MM(bank[:, :FT], wqf[kc][:, 128 * m:128 * (m + 1)],
                            XS[kc][:, :FT], start=(kc == 0), stop=(kc == DC - 1),
                            waits=(w_ if kc == 0 else ()), tag="fc")
                fcev.append(SC(lambda e, m=m, bank=bank: e.activation(
                    HB[m][:, :FT], bank[:, :FT], AF.Gelu), waits=[mt]))

            if tag == "a" and "h1" in dbg:
                for m in range(FCn):
                    dma(dbg["h1"][128 * m:128 * (m + 1), tofs:tofs + FT],
                        HB[m][:, :FT], sem=gq, waits=[fcev[m]], eng="gpsimd")
            if tag == "a":
                cut(1.0005 + 0.01 * t)
            m1 = m2 = None
            tprev = None
            for m in range(FCn):
                t2 = V(lambda e, m=m: e.tensor_mul(ssq[:, :FT], HB[m][:, :FT],
                                                   HB[m][:, :FT]),
                       waits=[fcev[m]] + ([tprev] if tprev else
                                          [(ve, ve.val)]) +
                             ([m2] if m2 else []))
                m1 = MM(PS[0:1, 2560:2560 + FT], ones_bf[:, 0:1], HB[m][:, :FT],
                        start=(m == 0), stop=(m == FCn - 1),
                        waits=[fcev[m]] + ([(ve, ve.val)] if m == 0 else []),
                        tag="hstats")
                m2 = MM(PS[0:1, 3072:3072 + FT], ones_bf[:, 0:1], ssq[:, :FT],
                        start=(m == 0), stop=(m == FCn - 1), waits=[t2],
                        tag="hstats")
                tprev = t2
            tA = V(lambda e: e.tensor_copy(rT("s", FT), PS[0:1, 2560:2560 + FT]),
                   waits=[m1, (ge, ge.val), (se, se.val)] + so.tok())
            tB = V(lambda e: e.tensor_copy(rT("q", FT), PS[0:1, 3072:3072 + FT]),
                   waits=[m2, tA])
            rmh = ln_rows(FT, [tB], F)

            if tag == "a":
                cut(1.0006 + 0.01 * t)
            tb1 = bcast(rB("sg", FT), BK[3], FT, rmh + [(ve, ve.val)])
            tb2 = bcast(rB("nm", FT), BK[4], FT, rmh + [(ve, ve.val)])
            tprev = None
            hnt = []
            for m in range(FCn):
                t1 = V(lambda e, m=m: e.tensor_mul(lnt1[:, :FT], HB[m][:, :FT],
                                                   BK[3][:, :FT]),
                       waits=[tb1, fcev[m]] + ([tprev] if tprev else
                                               gq.tok()))
                tprev = V(lambda e, m=m: e.tensor_add(HB[m][:, :FT], lnt1[:, :FT],
                                                      BK[4][:, :FT]),
                          waits=[tb2, t1])
                hnt.append(tprev)

            if tag == "a" and "hs1" in dbg:
                for m in range(FCn):
                    dma(dbg["hs1"][128 * m:128 * (m + 1), tofs:tofs + FT],
                        HB[m][:, :FT], sem=gq, waits=[hnt[m]], eng="gpsimd")
            if tag == "a":
                cut(1.0007 + 0.01 * t)
            new_store = []
            prev_ev = None
            for mo in range(DC):
                bank = BK[2]
                mt = None
                for m in range(FCn):
                    mt = MM(bank[:, :FT], wqp[m][:, 128 * mo:128 * (mo + 1)],
                            HB[m][:, :FT], start=(m == 0), stop=(m == FCn - 1),
                            waits=(([hnt[-1]] + ([prev_ev] if prev_ev else []))
                                   if m == 0 else ()), tag="proj")
                ev = SC(lambda e, bank=bank, mo=mo: e.activation(
                    EE[:, :FT], bank[:, :FT], AF.Identity, bias=b2h[:, mo:mo + 1],
                    scale=0.5), waits=[mt, (ve, ve.val)])
                prev_ev = ev
                ad = V(lambda e, mo=mo: e.tensor_add(XT[mo][:, :FT],
                                                     XT[mo][:, :FT], EE[:, :FT]),
                       waits=[ev])
                new_store.append(dma(xh[128 * mo:128 * (mo + 1), tofs:tofs + FT],
                                     XT[mo][:, :FT], waits=[ad], sem=sp))
            prev_store = new_store
            if tag == "a":
                cut(1.0 + 0.01 * (t + 1))
        if sink_dbg is not None:
            barrier()
            dma(sink_dbg, xh[:, :], sem=so, waits=sp.tok())
        f_sc.close()

    emit_ffn("a", src_fm=False, wqf=wqf, wqp=wqp, sink_dbg=dbg.get("x2"))
    barrier()
    ffn1_sc.close()
    cut(2)

    R1P = A(nc.sbuf_tensor("R1P", [1, T], BF16))
    R1NM = A(nc.sbuf_tensor("R1NM", [1, T], BF16))
    SYP = A(nc.sbuf_tensor("SYP", [1, T], BF16))
    SYNM = A(nc.sbuf_tensor("SYNM", [1, T], BF16))

    mg_sc = contextlib.ExitStack()
    MG = [mg_sc.enter_context(nc.sbuf_tensor(f"MG{i}", [128, T], BF16))
          for i in range(DC)]

    at_sc = contextlib.ExitStack()
    XS2 = [at_sc.enter_context(nc.sbuf_tensor(f"XS2{i}", [128, T], BF16))
           for i in range(DC)]
    ax_sc = contextlib.ExitStack()
    AXT = [ax_sc.enter_context(nc.sbuf_tensor(f"AXT{i}", [128, AT], F32))
           for i in range(DC)]

    prev = []
    for t in range(NAT):
        tofs = t * AT
        tls = [dma(AXT[i][:, :], xh[128 * i:128 * (i + 1), tofs:tofs + AT],
                   waits=prev + sp.tok()) for i in range(DC)]
        tq = stats2([AXT[i][:, :AT] for i in range(DC)], AT,
                    extra_waits=tls + [(ve, ve.val)])
        w = AT
        g1 = G(lambda e: e.tensor_scalar(rT("mu", w), rT("s", w), 1.0 / D,
                                         None, ALU.mult), waits=[tq])
        g2 = G(lambda e: e.tensor_mul(rT("var", w), rT("mu", w), rT("mu", w)),
               waits=[g1])
        g3 = V(lambda e: e.scalar_tensor_tensor(rT("var", w), rT("q", w),
                                                1.0 / D, rT("var", w),
                                                ALU.mult, ALU.subtract),
               waits=[g2, (ge, ge.val)])
        g4 = SC(lambda e: e.activation(rT("sg", w), rT("var", w), AF.Sqrt,
                                       bias=EPS, scale=1.0), waits=[g3])
        g5 = V(lambda e: e.reciprocal(rT("sg", w), rT("sg", w)), waits=[g4])
        g6 = G(lambda e: e.tensor_mul(rT("var", w), rT("var", w), rT("sg", w)),
               waits=[g5])
        g7 = G(lambda e: e.tensor_mul(rT("var", w), rT("var", w), rT("sg", w)),
               waits=[g6])
        g8 = SC(lambda e: e.activation(rT("nm", w), rT("var", w), AF.Sqrt,
                                       bias=EPS, scale=1.0), waits=[g7])
        g9 = V(lambda e: e.reciprocal(rT("nm", w), rT("nm", w)), waits=[g8])
        g10 = G(lambda e: e.tensor_mul(rT("nm", w), rT("nm", w), rT("sg", w)),
                waits=[g9])
        gs12 = G(lambda e: e.tensor_copy(rB("nm", w), rT("nm", w)), waits=[g10])
        g11 = G(lambda e: e.tensor_mul(rT("var", w), rT("mu", w), rT("nm", w)),
                waits=[g10])
        g12 = G(lambda e: e.tensor_scalar(rT("var", w), rT("var", w), -1.0,
                                          None, ALU.mult), waits=[g11])
        gn12 = G(lambda e: e.tensor_copy(rB("var", w), rT("var", w)),
                 waits=[g12])
        gr1 = G(lambda e, tofs=tofs: e.tensor_copy(R1P[0:1, tofs:tofs + w],
                                                   rT("sg", w)), waits=[g5])
        g13 = G(lambda e: e.tensor_mul(rT("q", w), rT("mu", w), rT("sg", w)),
                waits=[g5, tq])
        g14 = G(lambda e: e.tensor_scalar(rT("q", w), rT("q", w), -1.0, None,
                                          ALU.mult), waits=[g13])
        gr2 = G(lambda e, tofs=tofs: e.tensor_copy(R1NM[0:1, tofs:tofs + w],
                                                   rT("q", w)), waits=[g14])
        at = apply_ln([AXT[i][:, :AT] for i in range(DC)],
                      [XS2[i][:, tofs:tofs + AT] for i in range(DC)],
                      rB("nm", w), rB("var", w), AT,
                      waits=[gs12, gn12, gr1, gr2])
        prev = [at[-1]]
    ax_sc.close()
    cut(3)

    qk_sc = contextlib.ExitStack()
    wkv = [qk_sc.enter_context(nc.sbuf_tensor(f"wkv{i}", [128, 3 * D], BF16))
           for i in range(DC)]
    wstq = [qk_sc.enter_context(nc.sbuf_tensor(f"wstq{i}", [128, 3 * D // 2], F32))
            for i in range(2)]
    l1w = vcol("l1w", DC)
    l1b = vcol("l1b", DC)
    tq1 = dma(l1w[:, :], ln1w_in.rearrange("(c p) o -> p (c o)", p=128), slow=True)
    tq2 = dma(l1b[:, :], ln1b_in.rearrange("(c p) o -> p (c o)", p=128), slow=True)
    qbps = PS[:, 3104:3104 + 24]
    HWQ = 3 * D // 2
    qfold = []
    qbmm = []
    for cch in range(DC):
        for hf in range(2):
            buf = wstq[(2 * cch + hf) % 2]
            w_ = ([qfold[-2], qbmm[-2]] if len(qfold) >= 2 else list(prev))
            tl = dma(buf[:, :], wqkv_in[128 * cch:128 * (cch + 1),
                                        HWQ * hf:HWQ * (hf + 1)], waits=w_)
            bm = None
            for mo in range(12):
                gm = 12 * hf + mo
                bm = MM(qbps[:, gm:gm + 1], buf[:, 128 * mo:128 * (mo + 1)],
                        l1b[:, cch:cch + 1],
                        start=(cch == 0 and gm == 0),
                        stop=(cch == DC - 1 and gm == 23),
                        waits=[tl, tq2] + ([(ve, ve.val)]
                                           if cch == 0 and gm == 0 else []))
            qbmm.append(bm)
            qfold.append(V(lambda e, c=cch, hf=hf, buf=buf: e.tensor_scalar(
                wkv[c][:, HWQ * hf:HWQ * (hf + 1)], buf[:, :],
                l1w[:, c:c + 1], None, ALU.mult),
                waits=[tl, tq1, (pe, pe.val)]))
    qb = vcol("qb", 24)
    tqb = V(lambda e: e.tensor_copy(qb[:, 0:24], qbps[:, 0:24]),
            waits=now("tensor"))

    QR32 = qk_sc.enter_context(nc.sbuf_tensor("QR32", [128, T], F32))
    QBFT = qk_sc.enter_context(nc.sbuf_tensor("QBFT", [128, T], BF16))
    RT1 = qk_sc.enter_context(nc.sbuf_tensor("RT1", [128, 512], F32))
    RT2 = qk_sc.enter_context(nc.sbuf_tensor("RT2", [128, 512], F32))
    QMK = qk_sc.enter_context(nc.sbuf_tensor("QMK", [128, T], BF16))
    tab_sc = contextlib.ExitStack()
    tabA = tab_sc.enter_context(nc.sbuf_tensor("tabA", [128, T], F32))
    tabB = tab_sc.enter_context(nc.sbuf_tensor("tabB", [128, T], F32))

    NT512 = max(1, T // 512)
    TW = min(512, T)
    bank_ev = {0: None, 1: None}
    pay_toks = []

    def qkv_chunk(m, ev_fn):
        toks = []
        for tt in range(NT512):
            bi = tt % 2
            bank = BK[bi]
            w0 = [qfold[-1], tqb]
            if bank_ev[bi] is not None:
                w0.append(bank_ev[bi])
            mt = None
            for kc in range(DC):
                mt = MM(bank[:, :TW], wkv[kc][:, 128 * m:128 * (m + 1)],
                        XS2[kc][:, TW * tt:TW * (tt + 1)],
                        start=(kc == 0), stop=(kc == DC - 1),
                        waits=(w0 if kc == 0 else ()))
            ev = ev_fn(tt, bank, mt, m)
            bank_ev[bi] = ev
            toks.append(ev)
        return toks

    def emit_payload(m, src_toks):
        for pc in range(2):
            gm = G(lambda e, pc=pc: e.tensor_scalar(
                QMK[:, :T], QBFT[:, :T], pmk[:, pc:pc + 1], None, ALU.mult),
                waits=list(src_toks) + [tload[-1]] +
                      ([pay_toks[-1]] if pay_toks else []))
            pay_toks.append(dma(
                cc1_in[128 * m:128 * (m + 1), T * pc:T * (pc + 1)],
                QMK[:, :T], waits=[gm], sem=sp))

    for sk, ta_in, tb_in in ((0, taq_in, tbq_in), (1, tak_in, tbk_in)):
        tta = dma(tabA[:, :], ta_in)
        ttb = dma(tabB[:, :], tb_in)
        for jj in range(DC):
            m = 12 * (jj // 4) + 4 * sk + (jj % 4)

            def evq(tt, bank, mt, m):
                return SC(lambda e, tt=tt, bank=bank, m=m: e.activation(
                    QR32[:, TW * tt:TW * (tt + 1)], bank[:, :TW], AF.Identity,
                    bias=qb[:, m:m + 1]),
                    waits=[mt] + ([pay_toks[-1]] if pay_toks else []))
            evs = qkv_chunk(m, evq)
            tc = V(lambda e: e.tensor_copy(QBFT[:, :T], QR32[:, :T]),
                   waits=evs + ([pay_toks[-1]] if pay_toks else []))
            c3s = []
            prev_c2 = None
            for hf in range(max(1, T // 512)):
                HW_ = min(512, T)
                sl = slice(HW_ * hf, HW_ * (hf + 1))
                ps_qs = PS[:, 1024:1024 + HW_]
                mt = None
                for j in range((HW_ + 511) // 512):
                    jw = min(512, HW_ - 512 * j)
                    mt = MM(ps_qs[:, 512 * j:512 * j + jw], p2[:, :],
                            QBFT[:, HW_ * hf + 512 * j:HW_ * hf + 512 * j + jw],
                            start=True, stop=True,
                            waits=[tc] + ([prev_c2] if prev_c2 else []))
                c1 = V(lambda e, sl=sl, HW_=HW_: e.tensor_mul(
                    RT1[:, :HW_], QR32[:, sl], tabA[:, sl]),
                    waits=[tta, tc])
                c2 = V(lambda e, sl=sl, HW_=HW_, ps_qs=ps_qs: e.tensor_mul(
                    RT2[:, :HW_], ps_qs[:, :HW_], tabB[:, sl]),
                    waits=[ttb, mt, c1])
                prev_c2 = c2
                c3s.append(V(lambda e, sl=sl, HW_=HW_: e.tensor_add(
                    QBFT[:, sl], RT1[:, :HW_], RT2[:, :HW_]), waits=[c2]))
            emit_payload(m, c3s)
    for jj in range(DC):
        m = 12 * (jj // 4) + 8 + (jj % 4)

        def evv(tt, bank, mt, m):
            return SC(lambda e, tt=tt, bank=bank, m=m: e.activation(
                QBFT[:, TW * tt:TW * (tt + 1)], bank[:, :TW], AF.Identity,
                bias=qb[:, m:m + 1]),
                waits=[mt] + ([pay_toks[-1]] if pay_toks else []))
        evs = qkv_chunk(m, evv)
        emit_payload(m, evs)
    tab_sc.close()
    cut(4)

    t_cc1 = s.op("gpsimd", lambda e: e.collective_compute(
        "ReduceScatter", ALU.add, replica_groups=cfg.groups,
        ins=[cc1_in], outs=[cc1_out]),
        waits=pay_toks + [ccm.tok()], inc=ccm)
    if "qkv" in dbg:
        dma(dbg["qkv"], cc1_out, sem=gq, waits=[t_cc1], eng="gpsimd")
    cut(5)

    wg_sc = contextlib.ExitStack()
    WGB = [wg_sc.enter_context(nc.sbuf_tensor(f"WGB{i}", [128, D], BF16))
           for i in range(DC)]
    MGZ = wg_sc.enter_context(nc.sbuf_tensor("MGZ", [128, TW], F32))
    EXB = wg_sc.enter_context(nc.sbuf_tensor("EXB", [128, TW], F32))
    gbps = PS[:, 3104 + 24:3104 + 32]
    gfold = []
    gbmm = [qbmm[-1]]
    for cch in range(DC):
        buf = wstq[cch % 2]
        w_ = [qfold[-1], tqb, gbmm[-2 if len(gbmm) >= 2 else -1]]
        if len(gfold) >= 2:
            w_.append(gfold[-2])
        tl = dma(buf[:, :D], wg_in[128 * cch:128 * (cch + 1), :], waits=w_)
        bm = None
        for mo in range(DC):
            bm = MM(gbps[:, mo:mo + 1], buf[:, 128 * mo:128 * (mo + 1)],
                    l1b[:, cch:cch + 1], start=(cch == 0 and mo == 0),
                    stop=(cch == DC - 1 and mo == DC - 1),
                    waits=[tl])
        gbmm.append(bm)
        gfold.append(V(lambda e, c=cch, buf=buf: e.tensor_scalar(
            WGB[c][:, :], buf[:, :D], l1w[:, c:c + 1], None, ALU.mult),
            waits=[tl]))
    wgb = vcol("wgb", DC)
    twgb = V(lambda e: e.tensor_copy(wgb[:, 0:DC], gbps[:, 0:DC]),
             waits=now("tensor"))
    prev_mg = None
    for mo in range(DC):
        for tt in range(NT512):
            bi = tt % 2
            bank = BK[bi]
            w0 = [gfold[-1], twgb]
            if bank_ev[bi] is not None:
                w0.append(bank_ev[bi])
            mt = None
            for kc in range(DC):
                mt = MM(bank[:, :TW], WGB[kc][:, 128 * mo:128 * (mo + 1)],
                        XS2[kc][:, TW * tt:TW * (tt + 1)],
                        start=(kc == 0), stop=(kc == DC - 1),
                        waits=(w0 if kc == 0 else ()))
            e0 = SC(lambda e, bank=bank, mo=mo: e.activation(
                MGZ[:, :TW], bank[:, :TW], AF.Identity, bias=wgb[:, mo:mo + 1]),
                waits=[mt] + ([prev_mg] if prev_mg else []))
            e1 = SC(lambda e, bank=bank, mo=mo: e.activation(
                EXB[:, :TW], bank[:, :TW], AF.Exp, bias=wgb[:, mo:mo + 1]),
                waits=[mt])
            bank_ev[bi] = e1
            e2 = SC(lambda e: e.activation(EXB[:, :TW], EXB[:, :TW], AF.Ln,
                                           bias=1.0), waits=[e1])
            e3 = SC(lambda e: e.activation(EXB[:, :TW], EXB[:, :TW], AF.Tanh),
                    waits=[e2])
            prev_mg = V(lambda e, mo=mo, tt=tt: e.tensor_mul(
                MG[mo][:, TW * tt:TW * (tt + 1)], MGZ[:, :TW], EXB[:, :TW]),
                waits=[e0, e3])
    wg_sc.close()
    qk_sc.close()
    at_sc.close()
    cut(6)
    barrier()

    rt_sc = contextlib.ExitStack()
    DTm = rt_sc.enter_context(nc.sbuf_tensor("DTm", [128, HH * C], F32))
    KSm = rt_sc.enter_context(nc.sbuf_tensor("KSm", [128, HH * HD], F32))
    QSm = rt_sc.enter_context(nc.sbuf_tensor("QSm", [64, HH * C], F32))
    GRm = rt_sc.enter_context(nc.sbuf_tensor("GRm", [64, HH * HD], F32))
    gnv = rt_sc.enter_context(nc.sbuf_tensor("gnvS", [2, 2 * NPAIR], F32))
    Rst = rt_sc.enter_context(nc.sbuf_tensor("Rst", [64, HH * HD], F32))
    Rbf = rt_sc.enter_context(nc.sbuf_tensor("Rbf", [64, HH * HD], BF16))
    RTMP = rt_sc.enter_context(nc.sbuf_tensor("RTMP", [64, HH * HD], F32))
    qrS = [rt_sc.enter_context(nc.sbuf_tensor(f"qrS{i}", [64, HH * C], BF16))
           for i in range(2)]
    krS = [rt_sc.enter_context(nc.sbuf_tensor(f"krS{i}", [64, HH * C], BF16))
           for i in range(2)]
    vrS = [rt_sc.enter_context(nc.sbuf_tensor(f"vrS{i}", [64, HH * C], BF16))
           for i in range(2)]
    ktmS = [rt_sc.enter_context(nc.sbuf_tensor(f"ktmS{i}", [128, 512], BF16))
            for i in range(2)]
    vtmS = [rt_sc.enter_context(nc.sbuf_tensor(f"vtmS{i}", [128, 512], BF16))
            for i in range(2)]
    stbf = rt_sc.enter_context(nc.sbuf_tensor("stbf", [128, HH * C], BF16))
    qcS = rt_sc.enter_context(nc.sbuf_tensor("qcS", [64, HH * C], BF16))
    CXS = rt_sc.enter_context(nc.sbuf_tensor("CXS", [64, HH * C], BF16))
    RET = [rt_sc.enter_context(nc.sbuf_tensor(f"RET{i}", [128, S], BF16))
           for i in range(NPAIR)]

    for d_, so_ in ((DTm, dt_in), (KSm, ks_in), (QSm, qs_in), (GRm, gr_in),
                    (gnv, gnv_in)):
        dma(d_[:], so_, sem=lw)
    rz1 = V(lambda e: e.memzero(Rst[:, :]))
    rz2 = V(lambda e: e.memzero(Rbf[:, :]))

    rb_toks = {}

    def emit_readback(g, b):
        colsl = slice(C * g, C * (g + 1))
        toks = []
        for sec, dst in ((0, qrS[b]), (1, krS[b]), (2, vrS[b])):
            src = cc1_out[512 * sec:512 * (sec + 1), colsl]
            src = src.rearrange("(j p) c -> p j c", p=64)
            d3 = dst[:, :].rearrange("p (j c) -> p j c", c=C)
            toks.append(dma(d3, src, sem=rb,
                            waits=[t_cc1] + rb_toks.get((g - 2, "war"), [])))
        return toks

    rb_toks[0] = emit_readback(0, 0)
    if NCG > 1:
        rb_toks[1] = emit_readback(1, 1)
    cut(6.0001)

    ret_ev_all = []
    tr_prev = []
    st_prev = None
    for g in range(NCG):
        b = g % 2
        rbt = rb_toks[g]
        tks, ekt = [], []
        for hh in range(HH):
            w_ = ([rbt[1]] + tr_prev) if hh == 0 else (
                [ekt[-1]] if hh % 2 == 0 else [])
            tks.append(TRP(PSB[:, 64 * hh:64 * (hh + 1)],
                           krS[b][:, 128 * hh:128 * (hh + 1)],
                           identb[0:64, 0:64], waits=w_))
            if hh % 2 == 1:
                ekt.append(V(lambda e, b=b, hh=hh: e.tensor_tensor(
                    ktmS[b][:, 64 * (hh - 1):64 * (hh + 1)],
                    PSB[:, 64 * (hh - 1):64 * (hh + 1)],
                    KSm[:, 64 * (hh - 1):64 * (hh + 1)], ALU.mult),
                    waits=[tks[-1]] + (lw.tok() if g == 0 and hh == 1 else [])))
        tvs, evt = [], []
        for hh in range(HH):
            w_ = [rbt[2], ekt[-1]] if hh == 0 else (
                [evt[-1]] if hh % 2 == 0 else [])
            tvs.append(TRP(PSB[:, 512 + 64 * hh:512 + 64 * (hh + 1)],
                           vrS[b][:, 128 * hh:128 * (hh + 1)],
                           identb[0:64, 0:64], waits=w_))
            if hh % 2 == 1:
                evt.append(V(lambda e, b=b, hh=hh: e.tensor_copy(
                    vtmS[b][:, 64 * (hh - 1):64 * (hh + 1)],
                    PSB[:, 512 + 64 * (hh - 1):512 + 64 * (hh + 1)]),
                    waits=[tvs[-1]]))
        tr_prev = ekt + evt
        if g == 0:
            cut(6.0002)

        mm1, tst_t = [], []
        for hh in range(HH):
            w_ = ([rbt[0], rbt[1], (ve, ve.val), (se, se.val)] if hh == 0
                  else ([tst_t[-1]] if hh % 2 == 0 else []))
            mm1.append(MM(PS[:, 128 * hh:128 * (hh + 1)],
                          krS[b][:, 128 * hh:128 * (hh + 1)],
                          qrS[b][:, 128 * hh:128 * (hh + 1)], waits=w_))
            if hh % 2 == 1:
                tst_t.append(V(lambda e, hh=hh: e.tensor_tensor(
                    stbf[:, 128 * (hh - 1):128 * (hh + 1)],
                    PS[:, 128 * (hh - 1):128 * (hh + 1)],
                    DTm[:, 128 * (hh - 1):128 * (hh + 1)], ALU.mult),
                    waits=[mm1[-1]] + (lw.tok() if g == 0 and hh == 1 else [])))
        if g == 0:
            cut(6.0003)
        tqc = V(lambda e, b=b: e.tensor_tensor(qcS[:, :], qrS[b][:, :],
                                               QSm[:, :], ALU.mult),
                waits=[rbt[0], (pe, pe.val)] + (lw.tok() if g == 0 else []))

        rbf_w = rz2 if g == 0 else rupd_bf
        pair_rts = {}
        pair_ecx = {}
        ret_last = None
        for idx, pj in enumerate((0, 2, 1, 3)):
            hh0, hh1 = 2 * pj, 2 * pj + 1
            w_mm2 = [evt[pj // 2], tst_t[pj]]
            w_mcx = []
            if idx == 0:
                w_mcx = [tqc, rbf_w]
            if pj in (1, 3):
                w_mm2 += pair_rts[pj - 1]
                w_mcx = [pair_ecx[pj - 1]]
            m2a = MM(PS[0:64, 1024 + 128 * hh0:1024 + 128 * (hh0 + 1)],
                     vtmS[b][:, 64 * hh0:64 * (hh0 + 1)],
                     stbf[:, 128 * hh0:128 * (hh0 + 1)], waits=w_mm2)
            m2b = MM(PS[0:64, 1024 + 128 * hh1:1024 + 128 * (hh1 + 1)],
                     vtmS[b][:, 64 * hh1:64 * (hh1 + 1)],
                     stbf[:, 128 * hh1:128 * (hh1 + 1)])
            mca = MM(PS[0:64, 2048 + 128 * hh0:2048 + 128 * (hh0 + 1)],
                     Rbf[:, 64 * hh0:64 * (hh0 + 1)],
                     qcS[:, 128 * hh0:128 * (hh0 + 1)], waits=w_mcx)
            mcb = MM(PS[0:64, 2048 + 128 * hh1:2048 + 128 * (hh1 + 1)],
                     Rbf[:, 64 * hh1:64 * (hh1 + 1)],
                     qcS[:, 128 * hh1:128 * (hh1 + 1)])
            ecx_ = V(lambda e, pj=pj: e.tensor_copy(
                CXS[:, 256 * pj:256 * (pj + 1)],
                PS[0:64, 2048 + 256 * pj:2048 + 256 * (pj + 1)]), waits=[mcb])
            pair_ecx[pj] = ecx_
            rts = []
            for par, hh in ((0, hh0), (1, hh1)):
                rts.append(V(lambda e, pj=pj, par=par, hh=hh, g=g:
                             e.tensor_tensor(
                                 RET[pj][64 * par:64 * (par + 1),
                                         C * g:C * (g + 1)],
                                 PS[0:64, 1024 + 128 * hh:1024 + 128 * (hh + 1)],
                                 CXS[:, 128 * hh:128 * (hh + 1)], ALU.add),
                             waits=[ecx_, m2b]))
            pair_rts[pj] = rts
            ret_last = rts[-1]
            mcx_last = mcb
        ret_ev_all.append(ret_last)
        if g == 0:
            cut(6.0005)

        sme = PS[0:64, 3072:3072 + HH * HD].rearrange("p (h d) -> p h d", d=HD)
        r3 = lambda ap: ap.rearrange("p (h d) -> p h d", d=HD)
        rupd0 = V(lambda e: e.tensor_tensor(RTMP[:, :], Rst[:, :], GRm[:, :],
                                            ALU.mult),
                  waits=(lw.tok() + [rz1] if g == 0 else []))
        st_toks = []
        for pj in range(NPAIR):
            wst_ = []
            if pj == 0:
                wst_ = [ekt[-1], evt[-1]] + ([st_prev] if st_prev else [rz1])
            else:
                wst_ = [st_toks[-1]]
            ma = MM(PS[0:64, 3072 + 128 * pj:3072 + 128 * pj + 64],
                    ktmS[b][:, 128 * pj:128 * pj + 64],
                    vtmS[b][:, 128 * pj:128 * pj + 64], waits=wst_)
            mb = MM(PS[0:64, 3072 + 128 * pj + 64:3072 + 128 * (pj + 1)],
                    ktmS[b][:, 128 * pj + 64:128 * (pj + 1)],
                    vtmS[b][:, 128 * pj + 64:128 * (pj + 1)])
            st_toks.append(V(lambda e, pj=pj: e.tensor_tensor(
                r3(Rst[:, :])[:, 2 * pj:2 * pj + 2, :],
                r3(RTMP[:, :])[:, 2 * pj:2 * pj + 2, :],
                sme[:, 2 * pj:2 * pj + 2, :], ALU.add),
                waits=[mb, rupd0]))
        st_prev = st_toks[-1]
        rupd_bf = V(lambda e: e.tensor_copy(Rbf[:, :], Rst[:, :]),
                    waits=[st_toks[-1], mcx_last])
        cut(6.001 + 0.001 * g)
        if g + 2 < NCG:
            rb_toks[(g, "war")] = [mm1[-1], tqc, tks[-1], tvs[-1]]
            rb_toks[g + 2] = emit_readback(g + 2, b)

    cut(7)

    gncol = vcol("gncol", 8)
    SQD = rt_sc.enter_context(nc.sbuf_tensor("SQD", [128, S], BF16))
    gtoks = []
    for p_ in range(NPAIR):
        t1 = V(lambda e, p_=p_: e.reduce_sum(gncol[:, p_:p_ + 1],
                                             RET[p_][:, :S], axis=AX.X),
               waits=[ret_ev_all[-1]])
        t2 = SC(lambda e, p_=p_: e.activation(SQD[:, :S], RET[p_][:, :S],
                                              AF.Square,
                                              accum_out=gncol[:, 4 + p_:5 + p_]),
                waits=[ret_ev_all[-1]] + ([gtoks[-1]] if gtoks else []))
        gtoks.append(t2)
    mgn = MM(PS[0:2, 3104:3112], hsel[:, :], gncol[:, 0:8],
             waits=[gtoks[-1], (ve, ve.val)])
    GNS = vcol("gns", 16)
    tgn = V(lambda e: e.tensor_copy(GNS[0:2, 0:8], PS[0:2, 3104:3112]),
            waits=[mgn])
    nel = float(S * HD)
    ga1 = G(lambda e: e.tensor_scalar(GNS[0:2, 0:4], GNS[0:2, 0:4], 1.0 / nel,
                                      None, ALU.mult), waits=[tgn])
    ga2 = G(lambda e: e.tensor_scalar(GNS[0:2, 4:8], GNS[0:2, 4:8], 1.0 / nel,
                                      None, ALU.mult), waits=[tgn])
    ga3 = G(lambda e: e.tensor_mul(GNS[0:2, 8:12], GNS[0:2, 0:4],
                                   GNS[0:2, 0:4]), waits=[ga1])
    ga4 = G(lambda e: e.tensor_sub(GNS[0:2, 4:8], GNS[0:2, 4:8],
                                   GNS[0:2, 8:12]), waits=[ga2, ga3])
    ga5 = SC(lambda e: e.activation(GNS[0:2, 4:8], GNS[0:2, 4:8], AF.Sqrt,
                                    bias=EPS, scale=1.0), waits=[ga4])
    ga6 = V(lambda e: e.reciprocal(GNS[0:2, 4:8], GNS[0:2, 4:8]), waits=[ga5])
    ga7 = G(lambda e: e.tensor_mul(GNS[0:2, 4:8], GNS[0:2, 4:8],
                                   gnv[0:2, 0:NPAIR]), waits=[ga6] + lw.tok())
    ga8 = G(lambda e: e.tensor_mul(GNS[0:2, 8:12], GNS[0:2, 0:4],
                                   GNS[0:2, 4:8]), waits=[ga7])
    ga9 = G(lambda e: e.tensor_sub(GNS[0:2, 8:12], gnv[0:2, NPAIR:2 * NPAIR],
                                   GNS[0:2, 8:12]), waits=[ga8])
    mab = MM(PS[:, 3104:3112], parsel[:, :], GNS[0:2, 4:12], waits=[ga9])
    abcol = vcol("abcol", 8)
    tab_ = V(lambda e: e.tensor_copy(abcol[:, 0:8], PS[:, 3104:3112]),
             waits=[mab])
    gap = []
    for p_ in range(NPAIR):
        gap.append(V(lambda e, p_=p_: e.tensor_scalar(
            RET[p_][:, :S], RET[p_][:, :S], abcol[:, p_:p_ + 1],
            abcol[:, 4 + p_:5 + p_], ALU.mult, ALU.add),
            waits=[tab_] + ([gap[-1]] if gap else [])))

    CMK = rt_sc.enter_context(nc.sbuf_tensor("CMK", [128, T], BF16))
    cmks = [SQD[:, 0:T], CMK[:, :T]]
    p2_toks = []
    for s_ in range(2):
        for p_ in range(NPAIR):
            for slot, pc in ((0, 0), (512, 1)):
                scr = cmks[len(p2_toks) % 2]
                gm = G(lambda e, p_=p_, s_=s_, pc=pc, scr=scr: e.tensor_scalar(
                    scr, RET[p_][:, s_ * T:(s_ + 1) * T],
                    pmk[:, pc:pc + 1], None, ALU.mult),
                    waits=[gap[p_], tload[-1], gtoks[-1]] +
                          ([p2_toks[-2]] if len(p2_toks) >= 2 else []))
                p2_toks.append(dma(
                    cc2_in[1024 * s_ + slot + 128 * p_:
                           1024 * s_ + slot + 128 * (p_ + 1), :],
                    scr, waits=[gm], sem=sp))
    if "ret" in dbg:
        for p_ in range(NPAIR):
            dma(dbg["ret"][128 * p_:128 * (p_ + 1), :], RET[p_][:, :S],
                sem=gq, waits=[gap[p_]], eng="gpsimd")
    t_cc2 = s.op("gpsimd", lambda e: e.collective_compute(
        "ReduceScatter", ALU.add, replica_groups=cfg.groups,
        ins=[cc2_in], outs=[cc2_out]),
        waits=p2_toks + [ccm.tok()], inc=ccm)
    rt_sc.close()
    barrier()
    cut(8)

    go_sc = contextlib.ExitStack()
    RON = [go_sc.enter_context(nc.sbuf_tensor(f"RON{i}", [128, T], BF16))
           for i in range(DC)]
    WOB = [go_sc.enter_context(nc.sbuf_tensor(f"WOB{i}", [128, D], BF16))
           for i in range(DC)]
    wst2 = [go_sc.enter_context(nc.sbuf_tensor(f"wst2{i}", [128, D], F32))
            for i in range(2)]
    XLD2 = go_sc.enter_context(nc.sbuf_tensor("XLD2", [128, 512], F32))
    EE2 = go_sc.enter_context(nc.sbuf_tensor("EE2", [128, 512], F32))
    EE3 = go_sc.enter_context(nc.sbuf_tensor("EE3", [128, 512], F32))
    Y2T = [go_sc.enter_context(nc.sbuf_tensor(f"Y2T{i}", [128, 512], F32))
           for i in range(DC)]

    wo_toks = []
    for cch in range(DC):
        buf = wst2[cch % 2]
        w_ = [wo_toks[-2]] if len(wo_toks) >= 2 else []
        tl = dma(buf[:, :], wo_in[128 * cch:128 * (cch + 1), :], waits=w_,
                 sem=lw)
        wo_toks.append(SC(lambda e, c=cch, buf=buf: e.activation(
            WOB[c][:, :], buf[:, :], AF.Copy), waits=[tl]))

    ro_toks = []
    for r_ in range(DC):
        ro_toks.append(dma(RON[r_][:, :], cc2_out[128 * r_:128 * (r_ + 1), :],
                           sem=rb, waits=[t_cc2]))

    gt_toks = []
    for cch in range(DC):
        gt_toks.append(V(lambda e, c=cch: e.tensor_mul(
            MG[c][:, :T], MG[c][:, :T], RON[c][:, :T]),
            waits=[ro_toks[cch]] + ([gt_toks[-1]] if gt_toks else [])))

    NT512b = max(1, T // 512)
    prev_sp = []
    sy_toks = []
    for tt in range(NT512b):
        tofs = TW * tt
        tb1 = bcast(R1P[0:1, tofs:tofs + TW], BK[3], TW, [(ve, ve.val)])
        for mo in range(DC):
            bank = BK[mo % 2]
            w0 = [wo_toks[-1], gt_toks[-1]]
            if bank_ev[mo % 2] is not None:
                w0.append(bank_ev[mo % 2])
            mt = None
            for kc in range(DC):
                mt = MM(bank[:, :TW], WOB[kc][:, 128 * mo:128 * (mo + 1)],
                        MG[kc][:, tofs:tofs + TW],
                        start=(kc == 0), stop=False,
                        waits=(w0 if kc == 0 else ()))
            mt = MM(bank[:, :TW], ones_bf[0:1, 0:128],
                    R1NM[0:1, tofs:tofs + TW], start=False,
                    stop=True, waits=[mt])
            ev = SC(lambda e, bank=bank: e.activation(EE2[:, :TW], bank[:, :TW],
                                                      AF.Copy),
                    waits=[mt, (ve, ve.val)])
            bank_ev[mo % 2] = ev
            tl = dma(XLD2[:, :TW], xh[128 * mo:128 * (mo + 1), tofs:tofs + TW],
                     waits=prev_sp + [(ve, ve.val)])
            t0 = V(lambda e: e.tensor_mul(EE3[:, :TW], XLD2[:, :TW],
                                          BK[3][:, :TW]), waits=[tl, tb1])
            y2c = V(lambda e, mo=mo: e.tensor_add(Y2T[mo][:, :TW], EE2[:, :TW],
                                                  EE3[:, :TW]), waits=[t0, ev])
        tq = stats2([Y2T[i][:, :TW] for i in range(DC)], TW,
                    extra_waits=[(ve, ve.val)])
        prev_sp = [dma(y2h[128 * mo:128 * (mo + 1), tofs:tofs + TW],
                       Y2T[mo][:, :TW], waits=[tq], sem=sp)
                   for mo in range(DC)]
        rmyt = ln_rows(TW, [tq], D)
        sy_toks.append(G(lambda e, tofs=tofs: e.tensor_copy(
            SYP[0:1, tofs:tofs + TW], rB("sg", TW)), waits=rmyt))
        sy_toks.append(G(lambda e, tofs=tofs: e.tensor_copy(
            SYNM[0:1, tofs:tofs + TW], rB("nm", TW)), waits=rmyt))
    if "y2" in dbg:
        dma(dbg["y2"], y2h, sem=gq, waits=prev_sp, eng="gpsimd")
    go_sc.close()
    cut(9)

    f2_sc = contextlib.ExitStack()
    W1B = [f2_sc.enter_context(nc.sbuf_tensor(f"W1B{i}", [128, D], BF16))
           for i in range(DC)]
    W2B = [f2_sc.enter_context(nc.sbuf_tensor(f"W2B{i}", [128, D], BF16))
           for i in range(DC)]
    wst3 = [f2_sc.enter_context(nc.sbuf_tensor(f"wst3{i}", [128, D], F32))
            for i in range(2)]
    YT = [f2_sc.enter_context(nc.sbuf_tensor(f"YT{i}", [128, 512], F32))
          for i in range(DC)]
    YSB = [f2_sc.enter_context(nc.sbuf_tensor(f"YSB{i}", [128, 512], BF16))
           for i in range(DC)]
    H1B = [f2_sc.enter_context(nc.sbuf_tensor(f"H1B{i}", [128, 512], BF16))
           for i in range(DC)]
    XLD3 = f2_sc.enter_context(nc.sbuf_tensor("XLD3", [128, 512], F32))
    EE4 = f2_sc.enter_context(nc.sbuf_tensor("EE4", [128, 512], F32))

    l2w = vcol("l2w", DC)
    l2b = vcol("l2b", DC)
    w1bv = vcol("w1bv", DC)
    w2bv = vcol("w2bv", DC)
    b1t = vcol("b1t", DC)
    tl2w = dma(l2w[:, :], ln2w_in.rearrange("(c p) o -> p (c o)", p=128), sem=lw, slow=True)
    tl2b = dma(l2b[:, :], ln2b_in.rearrange("(c p) o -> p (c o)", p=128), sem=lw, slow=True)
    tw1b = dma(w1bv[:, :], w1b_in.rearrange("(c p) o -> p (c o)", p=128), sem=lw, slow=True)
    tw2b = dma(w2bv[:, :], w2b_in.rearrange("(c p) o -> p (c o)", p=128), sem=lw, slow=True)
    b1ps = PS[:, 3104:3104 + DC]
    w1f = []
    w1bm = []
    for cch in range(DC):
        buf = wst3[cch % 2]
        w_ = [w1f[-2], w1bm[-2]] if len(w1f) >= 2 else []
        tl = dma(buf[:, :], w1_in[128 * cch:128 * (cch + 1), :], waits=w_,
                 sem=lw)
        bm = None
        for mo in range(DC):
            bm = MM(b1ps[:, mo:mo + 1], buf[:, 128 * mo:128 * (mo + 1)],
                    l2b[:, cch:cch + 1], start=(cch == 0 and mo == 0),
                    stop=(cch == DC - 1 and mo == DC - 1),
                    waits=[tl, tl2b] + ([(ve, ve.val)]
                                        if cch == 0 and mo == 0 else []))
        w1bm.append(bm)
        w1f.append(V(lambda e, c=cch, buf=buf: e.tensor_scalar(
            W1B[c][:, :], buf[:, :], l2w[:, c:c + 1], None, ALU.mult),
            waits=[tl, tl2w]))
    tb1t = V(lambda e: e.tensor_copy(b1t[:, 0:DC], b1ps[:, 0:DC]),
             waits=now("tensor"))
    tb1t2 = V(lambda e: e.tensor_add(b1t[:, 0:DC], b1t[:, 0:DC], w1bv[:, 0:DC]),
              waits=[tb1t, tw1b])
    w2f = []
    for cch in range(DC):
        buf = wst3[cch % 2]
        w_ = [w1f[-1], w1bm[-1]] + ([w2f[-2]] if len(w2f) >= 2 else [])
        tl = dma(buf[:, :], w2_in[128 * cch:128 * (cch + 1), :], waits=w_,
                 sem=lw)
        w2f.append(SC(lambda e, c=cch, buf=buf: e.activation(
            W2B[c][:, :], buf[:, :], AF.Copy), waits=[tl]))

    prev_x3 = []
    for tt in range(NT512b):
        tofs = TW * tt
        tly = [dma(YT[i][:, :TW], y2h[128 * i:128 * (i + 1), tofs:tofs + TW],
                   waits=prev_x3 + [(ve, ve.val)] + sp.tok(), sem=lw)
               for i in range(DC)]
        ys = apply_ln([YT[i][:, :TW] for i in range(DC)],
                      [YSB[i][:, :TW] for i in range(DC)],
                      SYP[0:1, tofs:tofs + TW], SYNM[0:1, tofs:tofs + TW],
                      TW, waits=sy_toks[-2:] + tly + [(ve, ve.val)])
        h1e = []
        for mo in range(DC):
            bank = BK[mo % 2]
            w0 = [w1f[-1], tb1t2, ys[-1]]
            if bank_ev[mo % 2] is not None:
                w0.append(bank_ev[mo % 2])
            mt = None
            for kc in range(DC):
                mt = MM(bank[:, :TW], W1B[kc][:, 128 * mo:128 * (mo + 1)],
                        YSB[kc][:, :TW], start=(kc == 0), stop=(kc == DC - 1),
                        waits=(w0 if kc == 0 else ()))
            h1e.append(SC(lambda e, mo=mo, bank=bank: e.activation(
                H1B[mo][:, :TW], bank[:, :TW], AF.Gelu,
                bias=b1t[:, mo:mo + 1]), waits=[mt]))
            bank_ev[mo % 2] = h1e[-1]
        x3t = []
        prev_ev2 = None
        for mo in range(DC):
            bank = BK[2]
            mt = None
            for kc in range(DC):
                mt = MM(bank[:, :TW], W2B[kc][:, 128 * mo:128 * (mo + 1)],
                        H1B[kc][:, :TW], start=(kc == 0), stop=(kc == DC - 1),
                        waits=(([w2f[-1], h1e[-1]] +
                                ([prev_ev2] if prev_ev2 else []))
                               if kc == 0 else ()))
            ev = SC(lambda e, mo=mo, bank=bank: e.activation(
                EE4[:, :TW], bank[:, :TW], AF.Identity, bias=w2bv[:, mo:mo + 1]),
                waits=[mt, (ve, ve.val)])
            prev_ev2 = ev
            a1 = V(lambda e, mo=mo: e.tensor_add(YT[mo][:, :TW], YT[mo][:, :TW],
                                                 EE4[:, :TW]), waits=[ev])
            tl = dma(XLD3[:, :TW], xh[128 * mo:128 * (mo + 1), tofs:tofs + TW],
                     waits=[(ve, ve.val)], sem=lw)
            a2 = V(lambda e, mo=mo: e.tensor_add(YT[mo][:, :TW], YT[mo][:, :TW],
                                                 XLD3[:, :TW]), waits=[tl, a1])
            x3t.append(dma(xh[128 * mo:128 * (mo + 1), tofs:tofs + TW],
                           YT[mo][:, :TW], waits=[a2], sem=sp))
        prev_x3 = x3t
    barrier()
    if "x3" in dbg:
        dma(dbg["x3"], xh, sem=gq, waits=prev_x3, eng="gpsimd")
    f2_sc.close()
    mg_sc.close()
    cut(10)

    ffn2_sc = contextlib.ExitStack()
    wqf2 = [ffn2_sc.enter_context(nc.sbuf_tensor(f"wqf2_{i}", [128, F], BF16))
            for i in range(DC)]
    wqp2 = [ffn2_sc.enter_context(nc.sbuf_tensor(f"wqp2_{i}", [128, D], BF16))
            for i in range(FCn)]
    rl_toks = []
    for cch in range(DC):
        rl_toks.append(dma(wqf2[cch][:, :], wqf_sp[128 * cch:128 * (cch + 1), :],
                           sem=lw))
    for cch in range(FCn):
        rl_toks.append(dma(wqp2[cch][:, :], wqp_sp[128 * cch:128 * (cch + 1), :],
                           sem=lw))
    s.op("tensor", None, waits=rl_toks)
    s.op("vector", None, waits=rl_toks)
    emit_ffn("b", src_fm=True, wqf=wqf2, wqp=wqp2, init_waits=sp.tok() + lw.tok() + gq.tok())
    barrier()
    ffn2_sc.close()
    cut(11)

    o_sc = contextlib.ExitStack()
    OX = [o_sc.enter_context(nc.sbuf_tensor(f"OX{i}", [128, 512], F32))
          for i in range(DC)]
    OUTB = o_sc.enter_context(nc.sbuf_tensor("OUTB", [128, D], F32))
    out_toks = []
    n_out_dma = 0
    for tt in range(NT512b):
        tofs = TW * tt
        tls = [dma(OX[i][:, :TW], xh[128 * i:128 * (i + 1), tofs:tofs + TW],
                   waits=out_toks[-1:] + sp.tok(), sem=lw) for i in range(DC)]
        for sub in range(TW // 128):
            evs = []
            for mo in range(DC):
                psl = BK[2][:, 128 * (mo % 2):128 * (mo % 2 + 1)]
                pt = TRP(psl, OX[mo][:, 128 * sub:128 * (sub + 1)], ident[:],
                         waits=tls + ([evs[-1]] if len(evs) >= 1 else
                                      [(ve, ve.val)]))
                evs.append(V(lambda e, mo=mo, psl=psl: e.tensor_copy(
                    OUTB[:, 128 * mo:128 * (mo + 1)], psl),
                    waits=[pt] + (out_toks[-1:] if out_toks else [])))
            out_toks.append(dma(out_t[tofs + 128 * sub:tofs + 128 * (sub + 1), :],
                                OUTB[:, :], waits=[evs[-1]], sem=so))
            n_out_dma += 1
    s.op("sync", None, waits=[out_toks[-1]])
    s.op("gpsimd", None, waits=[out_toks[-1]])
    o_sc.close()

    with nc.Block() as block:
        s.emit(block)
    st_.close()
    return nc



def _prep_inputs(inputs, cfg):
    T, S, F = cfg.T, cfg.S, cfg.F
    f32 = np.float32
    x = np.asarray(inputs["L0_attn_x"], f32)
    B = x.shape[0]
    wfc = np.ascontiguousarray(np.asarray(inputs["L0_ffn_W_fc"], f32)[:F, :].T)
    wproj = np.ascontiguousarray(np.asarray(inputs["L0_ffn_W_proj"], f32)[:, :F].T)
    flnw = np.asarray(inputs["L0_ffn_ln_w"], f32)[:F].reshape(F, 1)
    flnb = np.asarray(inputs["L0_ffn_ln_b"], f32)[:F].reshape(F, 1)
    wqkv = np.asarray(inputs["L0_attn_wqkv_w"], f32)
    perm = [192 * h + 64 * sec + d
            for blk in range(2)
            for sec in range(3)
            for h in range(8 * blk, 8 * blk + 8)
            for d in range(64)]
    wqkv_p = np.ascontiguousarray(wqkv[:, perm])
    col = lambda nm: np.asarray(inputs[nm], f32).reshape(D, 1)
    gn_w = np.asarray(inputs["L0_attn_gn_w"], f32)
    gn_b = np.asarray(inputs["L0_attn_gn_b"], f32)
    A_q, B_q, A_k, B_k = make_xpos_tables(S)
    rowsel = np.arange(128) % 64

    common = dict(
        wfc=wfc, wproj=wproj, flnw=flnw, flnb=flnb, wqkv=wqkv_p,
        ln1w=col("L0_attn_ln1_w"), ln1b=col("L0_attn_ln1_b"),
        wg=np.asarray(inputs["L0_attn_wg_w"], f32),
        wo=np.asarray(inputs["L0_attn_wo_w"], f32),
        w1=np.asarray(inputs["L0_attn_w1_w"], f32),
        w1b=col("L0_attn_w1_b"),
        w2=np.asarray(inputs["L0_attn_w2_w"], f32),
        w2b=col("L0_attn_w2_b"),
        ln2w=col("L0_attn_ln2_w"), ln2b=col("L0_attn_ln2_b"),
        identm=np.eye(128, dtype=f32),
        identbm=np.eye(128, dtype=f32).astype(ml_dtypes.bfloat16),
        p2m=make_p2swap(),
        onesm=np.ones((128, 128), f32),
        hselm=make_hsel(), parselm=make_parsel(),
    )

    in_maps = []
    for c in range(cfg.n_cores):
        b, c2 = c // 2, c % 2
        P0 = c2 * T
        heads = list(range(HH * c2, HH * (c2 + 1)))
        DT, KS, QS, GR = make_ret_tables(heads)
        gnv = np.zeros((2, 2 * NPAIR), f32)
        for p in range(NPAIR):
            for par in range(2):
                gnv[par, p] = gn_w[heads[2 * p + par]]
                gnv[par, NPAIR + p] = gn_b[heads[2 * p + par]]
        m = dict(common)
        m["x_in"] = np.ascontiguousarray(x[b, P0:P0 + T, :])
        m["taq"] = np.ascontiguousarray(A_q[P0:P0 + T, rowsel].T)
        m["tbq"] = np.ascontiguousarray(B_q[P0:P0 + T, rowsel].T)
        m["tak"] = np.ascontiguousarray(A_k[P0:P0 + T, rowsel].T)
        m["tbk"] = np.ascontiguousarray(B_k[P0:P0 + T, rowsel].T)
        m["dtm"], m["ksm"], m["qsm"], m["grm"] = DT, KS, QS, GR
        m["gnv"] = gnv
        pm = np.zeros((128, 2), f32)
        pm[:, c2] = 1.0
        m["pmask"] = pm
        in_maps.append(m)
    return in_maps


def kernel(**inputs):
    cfg = CFG()
    in_maps = _prep_inputs(inputs, cfg)
    nc = build(cfg)
    res = run_bass_kernel_spmd(nc, in_maps, core_ids=list(range(cfg.n_cores)))
    B = np.asarray(inputs["L0_attn_x"]).shape[0]
    out = np.empty((B, cfg.S, D), np.float32)
    for b in range(B):
        out[b, :cfg.T] = res.results[2 * b]["out"]
        out[b, cfg.T:] = res.results[2 * b + 1]["out"]
    return out
